# version 01 (final)
# baseline (speedup 1.0000x reference)
import numpy as np
from contextlib import ExitStack

import concourse.bass as bass
import concourse.mybir as mybir
from concourse.bass import IndirectOffsetOnAxis
from concourse.bass_utils import run_bass_kernel_spmd

F32 = mybir.dt.float32
BF16 = mybir.dt.bfloat16
I32 = mybir.dt.int32
U32 = mybir.dt.uint32
AF = mybir.ActivationFunctionType
ALU = mybir.AluOpType
AX = mybir.AxisListType

D = 1024
S = 2048
H = 16
DH = 64
NPAIR = 8
E = 16
II = 1024
RSCALE = 2.826
EPS = 1e-5
NM = 100000
SCALE = 0.125

SEQS = ("pe", "dve", "act", "pool", "sp")
SEQ_OF = {"pe": "pe", "dve": "dve", "act": "act", "pool": "pool", "sp": "sp",
          "qsp": "sp", "qact": "act", "qpool": "pool"}


class B:
    def __init__(self):
        self.nc = bass.Bass("TRN2", target_bir_lowering=False)
        self.es = ExitStack()
        self.plan = {s: [] for s in SEQS}
        self.sem = {}
        self.cnt = {}
        self.wm = {}
        self.lw = {}
        self.rd = {}
        self.pend = {k: [] for k in SEQ_OF}
        self.dsem_seq = {}
        for k in ("pe", "dve", "act", "pool"):
            self.sem[k] = self.es.enter_context(self.nc.semaphore("sem_" + k))
            self.cnt[k] = 0
        self.scrap = self.es.enter_context(self.nc.sbuf_tensor("scrap", [1, 8], mybir.dt.float32))

    def get_sem(self, key):
        if key not in self.sem:
            self.sem[key] = self.es.enter_context(self.nc.semaphore("ds_" + key.replace(":", "_")))
            self.cnt[key] = 0
        return self.sem[key]

    def sb(self, name, shape, dtype=F32):
        return self.es.enter_context(self.nc.sbuf_tensor(name, shape, dtype))

    def psum(self, name, shape):
        return self.es.enter_context(self.nc.psum_tensor(name, shape, F32))

    def dram(self, name, shape, dtype=F32, kind="Internal"):
        return self.nc.dram_tensor(name, shape, dtype, kind=kind).ap()

    def issue(self, ekey, fn, reads=(), writes=(), war=(), inc=True, dsem=None):
        seq = SEQ_OF[ekey]
        is_dma = ekey.startswith("q")
        if is_dma:
            skey = dsem if dsem is not None else (writes[0] if writes else "dmisc")
            self.get_sem(skey)
            self.dsem_seq[skey] = seq
        else:
            skey = ekey
        need = {}

        def _need(pt):
            k, v = pt
            if need.get(k, 0) < v:
                need[k] = v

        for bk in reads:
            p = self.lw.get(bk)
            if p:
                _need(p)
        for bk in war:
            p = self.lw.get(bk)
            if p:
                _need(p)
            for rk, rv in self.rd.get(bk, {}).items():
                _need((rk, rv))
        for bk in writes:
            p = self.lw.get(bk)
            if p:
                _need(p)
            for rk, rv in self.rd.get(bk, {}).items():
                _need((rk, rv))
        ops = self.plan[seq]
        for k in sorted(need):
            v = need[k]
            if self.wm.get((seq, k), 0) >= v:
                continue
            s = self.sem[k]
            ops.append(lambda eng, s=s, v=v: eng.wait_ge(s, v))
            self.wm[(seq, k)] = v
        amount = 16 if is_dma else 1
        if inc:
            self.cnt[skey] += amount
            val = self.cnt[skey]
            sem = self.sem[skey]
            ops.append(lambda eng, fn=fn, sem=sem, a=amount: fn(eng).then_inc(sem, a))
            pt = (skey, val)
            for bk in writes:
                self.lw[bk] = pt
                self.rd[bk] = {}
            for bk in reads:
                self.rd.setdefault(bk, {})[skey] = val
            for bk in self.pend[ekey]:
                self.rd.setdefault(bk, {})[skey] = val
            self.pend[ekey] = []
        else:
            assert not writes
            ops.append(lambda eng, fn=fn: fn(eng))
            self.pend[ekey].extend(reads)

    def phase(self, old, new):
        self.issue("dve", lambda v: v.memset(self.scrap[0:1, 0:4], 0.0), war=list(old), writes=list(new) + ["scrapd"])

    def barrier_all_dma(self):
        for skey, seq in self.dsem_seq.items():
            v = self.cnt[skey]
            if v and self.wm.get((seq, skey), 0) < v:
                s = self.sem[skey]
                self.plan[seq].append(lambda eng, s=s, v=v: eng.wait_ge(s, v))
                self.wm[(seq, skey)] = v


def build(debug=False):
    b = B()
    nc = b.nc

    xin = b.dram("xin", [D, S], F32, kind="ExternalInput")
    minds = b.dram("minds", [2, NM], I32, kind="ExternalInput")
    wqT = b.dram("wqT", [D, D], F32, kind="ExternalInput")
    wkT = b.dram("wkT", [D, D], F32, kind="ExternalInput")
    wvT = b.dram("wvT", [D, D], F32, kind="ExternalInput")
    woT = b.dram("woT", [D, D], F32, kind="ExternalInput")
    gwT = b.dram("gwT", [D, E], F32, kind="ExternalInput")
    ebias = b.dram("ebias", [1, E], F32, kind="ExternalInput")
    wgT = b.dram("wgT", [E, D, II], F32, kind="ExternalInput")
    wuT = b.dram("wuT", [E, D, II], F32, kind="ExternalInput")
    wdT = b.dram("wdT", [E, II, D], F32, kind="ExternalInput")
    sgT = b.dram("sgT", [D, II], F32, kind="ExternalInput")
    suT = b.dram("suT", [D, II], F32, kind="ExternalInput")
    sdT = b.dram("sdT", [II, D], F32, kind="ExternalInput")
    y_fm = b.dram("y_fm", [D, S], F32, kind="ExternalOutput")

    MLEN = S * S + 128
    Md = b.dram("Md", [1, MLEN], BF16)
    Md8 = b.dram("Md8", [2, MLEN], BF16)
    x2d = b.dram("x2d", [D, S], F32, kind="ExternalOutput" if debug else "Internal")
    rowsd = b.dram("rowsd", [4, S], F32)
    cmbd = b.dram("cmbd", [E, S], F32, kind="ExternalOutput" if debug else "Internal")

    dbg = {}
    if debug:
        dbg["xn"] = b.dram("dbg_xn", [D, S], F32, kind="ExternalOutput")
        dbg["q0"] = b.dram("dbg_q0", [128, S], F32, kind="ExternalOutput")
        dbg["k0"] = b.dram("dbg_k0", [128, S], F32, kind="ExternalOutput")
        dbg["v0"] = b.dram("dbg_v0", [128, 16 * 130], F32, kind="ExternalOutput")
        dbg["z0"] = b.dram("dbg_z0", [128, S], F32, kind="ExternalOutput")
        dbg["sig"] = b.dram("dbg_sig", [E, S], F32, kind="ExternalOutput")
        dbg["x2n"] = b.dram("dbg_x2n", [D, S], F32, kind="ExternalOutput")
        dbg["p0"] = b.dram("dbg_p0", [128, 512], F32, kind="ExternalOutput")

    xn = b.sb("xn", [128, 8 * S])
    wb = [b.sb(f"wb{i}", [128, 1024]) for i in range(6)]
    bc0 = b.sb("bc0", [128, S])
    bc1 = b.sb("bc1", [128, S])
    U = b.sb("U", [128, 14112])
    q_sb = U[:, 0:2048]
    k_sb = U[:, 2048:4096]
    z_sb = U[:, 4096:6144]
    v_sb = U[:, 6144:8224]
    pex = [U[:, 8224 + i * 512: 8736 + i * 512] for i in range(4)]
    ost = [U[:, 10272 + i * 512: 10784 + i * 512] for i in range(2)]
    rowv = [U[0:1, 11296:13344], U[32:33, 11296:13344]]
    m_sb = [U[:, 13344 + i * 256: 13600 + i * 256].bitcast(BF16) for i in range(3)]
    xs = [U[:, 0:2048], U[:, 2048:4096]]
    sq = U[:, 4096:6144]
    h_sb = U[:, 0:4096]
    acc = U[:, 4096:8192]
    cb = [U[:, 8192 + i * 512: 8704 + i * 512] for i in range(2)]
    LNKEYS = ["xs0", "xs1", "sq"]
    ATTNKEYS = (["q", "k", "z", "v", "vones", "vones2", "ost0", "ost1", "row0", "row1"]
                + [f"pex{i}" for i in range(4)] + [f"m{i}" for i in range(3)])
    FFNKEYS = [f"h{ic}" for ic in range(8)] + ["acc", "cb0", "cb1"]
    onec = b.sb("onec", [128, 1])
    zc = b.sb("zc", [128, 1], BF16)
    oneb = b.sb("oneb", [128, 1024], BF16)
    mcb = [b.sb(f"mcb{i}", [128, 1024], BF16) for i in range(4)]
    idm = b.sb("idm", [128, 128])
    st = [b.sb(f"st{i}", [128, 16]) for i in range(4)]
    mi_k = b.sb("mi_k", [128, 782], I32)
    mi_q = b.sb("mi_q", [128, 782], I32)
    sig = b.sb("sig", [E, S])
    selT = b.sb("selT", [128, 256])
    sigT = b.sb("sigT", [128, 256])
    rank = b.sb("rank", [128, 256])
    tmp3 = b.sb("tmp3", [128, 256])
    cmbT = b.sb("cmbT", [128, 256])
    cst = b.sb("cst", [16, 128])

    P = [b.psum(f"P{i}", [128, 1024]) for i in range(4)]

    def pk(i, half):
        return f"P{i}{half}"

    def pap(i, half, rows=128, cols=512):
        off = 0 if half == "a" else 512
        return P[i][0:rows, off:off + cols]

    I = b.issue

    I("dve", lambda v: v.memset(onec[:], 1.0), writes=["onec"])
    I("dve", lambda v: v.memset(zc[:], 0.0), writes=["zc"])
    I("dve", lambda v: v.memset(oneb[:], 1.0), writes=["oneb"])
    I("pool", lambda g: g.iota(idm[:], [[1, 128]], base=0, channel_multiplier=-1,
                               allow_small_or_imprecise_dtypes=True), writes=["idm"])
    I("dve", lambda v: v.tensor_single_scalar(idm[:], idm[:], 0.0, ALU.is_equal),
      reads=["idm"], writes=["idm"])

    I("qsp", lambda s: s.dma_start(out=mi_q[:, 0:781],
                                   in_=minds[0:1, 0:99968].rearrange("a (p f) -> (a p) f", p=128)),
      writes=["mi_q"])
    I("qsp", lambda s: s.dma_start(out=mi_q[0:32, 781:782],
                                   in_=minds[0:1, 99968:100000].rearrange("a (p f) -> (a p) f", p=32)),
      writes=["mi_q2"])
    I("qsp", lambda s: s.dma_start(out=mi_k[:, 0:781],
                                   in_=minds[1:2, 0:99968].rearrange("a (p f) -> (a p) f", p=128)),
      writes=["mi_k"])
    I("qsp", lambda s: s.dma_start(out=mi_k[0:32, 781:782],
                                   in_=minds[1:2, 99968:100000].rearrange("a (p f) -> (a p) f", p=32)),
      writes=["mi_k2"])
    I("dve", lambda v: v.memset(mi_k[32:64, 781:782], 0), writes=["mi_kp"])
    I("dve", lambda v: v.memset(mi_k[64:128, 781:782], 0), writes=["mi_kp2"])
    I("dve", lambda v: v.memset(mi_q[32:64, 781:782], 0), writes=["mi_qp"])
    I("dve", lambda v: v.memset(mi_q[64:128, 781:782], 0), writes=["mi_qp2"])
    I("dve", lambda v: v.tensor_scalar(mi_k[:], mi_k[:], S, None, ALU.mult),
      reads=["mi_k", "mi_k2", "mi_kp", "mi_kp2"], writes=["mi_k"])
    I("dve", lambda v: v.tensor_tensor(mi_k[:], mi_k[:], mi_q[:], ALU.add),
      reads=["mi_k", "mi_q", "mi_q2", "mi_qp", "mi_qp2"], writes=["mi_k"])
    I("dve", lambda v: v.memset(mi_k[32:64, 781:782], S * S), war=["mi_k"], writes=["flatpad"])
    I("dve", lambda v: v.memset(mi_k[64:128, 781:782], S * S), war=["mi_k"], writes=["flatpad2"])
    for c in range(2):
        for i in range(32):
            I("qact", lambda s, c=c, i=i: s.dma_start(
                out=Md8[c:c + 1, i * 131072:(i + 1) * 131072].rearrange("a (p f) -> (a p) f", p=128),
                in_=oneb[:]), reads=["oneb"], writes=[f"M8c{c}"], dsem=f"M8f{c}")
    for j in range(782):
        c = j % 2
        I("qpool", lambda g, j=j, c=c: g.indirect_dma_start(
            out=Md8[:, :],
            out_offset=IndirectOffsetOnAxis(ap=mi_k[:, j:j + 1].bitcast(U32), axis=1),
            in_=zc[:], in_offset=None, element_offset=c * MLEN),
          reads=["mi_k", "flatpad", "flatpad2", "zc"],
          writes=[f"M8c{c}"], dsem=f"Msc{c}")
    for i in range(32):
        pa = (i % 2) * 2
        I("qact", lambda s, i=i, pa=pa: s.dma_start(
            out=mcb[pa][:],
            in_=Md8[0:1, i * 131072:(i + 1) * 131072].rearrange("a (p f) -> (a p) f", p=128)),
          reads=["M8c0"], writes=[f"mcb{pa}"], dsem=f"mc{pa}")
        I("qact", lambda s, i=i, pa=pa: s.dma_start(
            out=mcb[pa + 1][:],
            in_=Md8[1:2, i * 131072:(i + 1) * 131072].rearrange("a (p f) -> (a p) f", p=128)),
          reads=["M8c1"], writes=[f"mcb{pa + 1}"], dsem=f"mc{pa + 1}")
        I("dve", lambda v, pa=pa: v.tensor_tensor(mcb[pa][:], mcb[pa][:], mcb[pa + 1][:], ALU.min),
          reads=[f"mcb{pa}", f"mcb{pa + 1}"], writes=[f"mcb{pa}"])
        I("qact", lambda s, i=i, pa=pa: s.dma_start(
            out=Md[0:1, i * 131072:(i + 1) * 131072].rearrange("a (p f) -> (a p) f", p=128),
            in_=mcb[pa][:]), reads=[f"mcb{pa}"], writes=[f"Mcs{i}"], dsem=f"Mdst{pa}")
    I("pool", lambda g: g.memset(b.scrap[0:1, 4:8], 0.0), reads=[f"Mcs{i}" for i in range(32)], writes=["Md", "scrapp"])

    def layer_norm(src_chunk_ap, src_keys, write_x2):
        for dc in range(8):
            par = dc % 2
            I("qsp", lambda s, dc=dc, par=par: s.dma_start(out=xs[par][:], in_=src_chunk_ap(dc)),
              reads=src_keys(dc), writes=[f"xs{par}"])
            I("act", lambda a, par=par: a.activation(sq[:], xs[par][:], AF.Square),
              reads=[f"xs{par}"], writes=["sq"])
            for tt in range(16):
                first = (dc == 0 and tt == 0)
                last = (dc == 7 and tt == 15)
                I("pe", lambda t, par=par, tt=tt, first=first, last=last: t.matmul(
                    pap(2, "a", 128, 16)[:, tt:tt + 1], xs[par][:, tt * 128:(tt + 1) * 128],
                    onec[:], start=first, stop=last),
                  reads=[f"xs{par}", "onec"], war=[pk(2, "a")] if first else (),
                  writes=[pk(2, "a")] if last else (), inc=(tt == 15))
            for tt in range(16):
                first = (dc == 0 and tt == 0)
                last = (dc == 7 and tt == 15)
                I("pe", lambda t, tt=tt, first=first, last=last: t.matmul(
                    pap(2, "b", 128, 16)[:, tt:tt + 1], sq[:, tt * 128:(tt + 1) * 128],
                    onec[:], start=first, stop=last),
                  reads=["sq", "onec"], war=[pk(2, "b")] if first else (),
                  writes=[pk(2, "b")] if last else (), inc=(tt == 15))
        I("dve", lambda v: v.tensor_scalar(st[0][:], pap(2, "a", 128, 16), -1.0 / D, None, ALU.mult),
          reads=[pk(2, "a")], writes=["st0"])
        I("dve", lambda v: v.tensor_scalar(st[1][:], pap(2, "b", 128, 16), 1.0 / D, None, ALU.mult),
          reads=[pk(2, "b")], writes=["st1"])
        I("dve", lambda v: v.tensor_tensor(st[2][:], st[0][:], st[0][:], ALU.mult),
          reads=["st0"], writes=["st2"])
        I("dve", lambda v: v.tensor_tensor(st[1][:], st[1][:], st[2][:], ALU.subtract),
          reads=["st1", "st2"], writes=["st1"])
        I("dve", lambda v: v.tensor_scalar(st[1][:], st[1][:], EPS, None, ALU.add),
          reads=["st1"], writes=["st1"])
        I("act", lambda a: a.activation(st[1][:], st[1][:], AF.Sqrt),
          reads=["st1"], writes=["st1"])
        I("dve", lambda v: v.reciprocal(st[1][:], st[1][:]), reads=["st1"], writes=["st1"])
        I("dve", lambda v: v.tensor_tensor(st[0][:], st[0][:], st[1][:], ALU.mult),
          reads=["st0", "st1"], writes=["st0"])
        for srci, dsti in ((1, 0), (0, 1)):
            I("pe", lambda t, srci=srci: t.transpose(pap(3, "a", 16, 128), st[srci][:], idm[:]),
              reads=[f"st{srci}", "idm"], war=[pk(3, "a")], writes=[pk(3, "a")])
            I("dve", lambda v: v.tensor_copy(cst[:], pap(3, "a", 16, 128)),
              reads=[pk(3, "a")], writes=["cst"])
            I("qact", lambda s, dsti=dsti: s.dma_start(
                out=rowsd[dsti:dsti + 1, :].rearrange("a (t p) -> (a t) p", p=128),
                in_=cst[:]), reads=["cst"], writes=[f"rowsd{dsti}"])
        I("qact", lambda s: s.dma_start(out=bc0[:], in_=rowsd[0:1, :].partition_broadcast(128)),
          reads=["rowsd0"], writes=["bc0"])
        I("qact", lambda s: s.dma_start(out=bc1[:], in_=rowsd[1:2, :].partition_broadcast(128)),
          reads=["rowsd1"], writes=["bc1"])
        for dc in range(8):
            par = dc % 2
            I("qsp", lambda s, dc=dc, par=par: s.dma_start(out=xs[par][:], in_=src_chunk_ap(dc)),
              reads=src_keys(dc), writes=[f"xs{par}"])
            xnk = f"xn{dc}"
            I("dve", lambda v, dc=dc, par=par: v.tensor_tensor(
                xn[:, dc * S:(dc + 1) * S], xs[par][:], bc0[:], ALU.mult),
              reads=[f"xs{par}", "bc0"], writes=[xnk])
            I("dve", lambda v, dc=dc: v.tensor_tensor(
                xn[:, dc * S:(dc + 1) * S], xn[:, dc * S:(dc + 1) * S], bc1[:], ALU.add),
              reads=[xnk, "bc1"], writes=[xnk])
            if write_x2:
                I("dve", lambda v, dc=dc, par=par: v.tensor_tensor(
                    sq[:], xs[par][:], xn[:, dc * S:(dc + 1) * S], ALU.add),
                  reads=[f"xs{par}", xnk], writes=["sq"])
                I("qsp", lambda s, dc=dc: s.dma_start(out=x2d[dc * 128:(dc + 1) * 128, :], in_=sq[:]),
                  reads=["sq"], writes=[f"x2:{dc}:{tc}" for tc in range(4)], dsem=f"x2i{dc}")

    layer_norm(lambda dc: xin[dc * 128:(dc + 1) * 128, :], lambda dc: [], write_x2=True)

    if debug:
        for dc in range(8):
            I("qact", lambda s, dc=dc: s.dma_start(out=dbg["xn"][dc * 128:(dc + 1) * 128, :],
                                                   in_=xn[:, dc * S:(dc + 1) * S]),
              reads=[f"xn{dc}"], writes=[f"dbgxn{dc}", "dbgc"], dsem="dbgc")

    b.phase(LNKEYS, ATTNKEYS)
    I("dve", lambda v: v.memset(v_sb.rearrange("p (t c) -> p t c", c=130)[:, :, 64:65], 1.0),
      writes=["vones"])
    I("dve", lambda v: v.memset(v_sb.rearrange("p (t c) -> p t c", c=130)[:, :, 129:130], 1.0),
      writes=["vones2"])
    xnkeys = [f"xn{dc}" for dc in range(8)]
    for hp in range(NPAIR):
        I("qsp", lambda s, hp=hp: s.dma_start(
            out=wb[0][:].rearrange("p (c o) -> p c o", o=128),
            in_=bass.AP(wqT.tensor, hp * 128, [[1024, 128], [131072, 8], [1, 128]])),
          writes=["wb0"])
        I("qsp", lambda s, hp=hp: s.dma_start(
            out=wb[1][:].rearrange("p (c o) -> p c o", o=128),
            in_=bass.AP(wkT.tensor, hp * 128, [[1024, 128], [131072, 8], [1, 128]])),
          writes=["wb1"])
        I("qsp", lambda s, hp=hp: s.dma_start(
            out=wb[2][:].rearrange("p (c o) -> p c o", o=128),
            in_=bass.AP(wvT.tensor, hp * 128, [[1024, 128], [131072, 8], [1, 128]])),
          writes=["wb2"])
        I("qsp", lambda s, hp=hp: s.dma_start(out=wb[3][:], in_=woT[hp * 128:(hp + 1) * 128, :]),
          writes=["wo"])
        for name, wkey, dest, dkey in (("q", "wb0", q_sb, "q"), ("k", "wb1", k_sb, "k")):
            for qc in range(4):
                slot = (0, "a") if qc % 2 == 0 else (0, "b")
                for dc in range(8):
                    I("pe", lambda t, wkey=wkey, dc=dc, qc=qc, slot=slot: t.matmul(
                        pap(*slot), wb[0 if wkey == "wb0" else 1][:, dc * 128:(dc + 1) * 128],
                        xn[:, dc * S + qc * 512: dc * S + qc * 512 + 512],
                        start=(dc == 0), stop=(dc == 7)),
                      reads=[wkey] + xnkeys, war=[pk(*slot)] if dc == 0 else (),
                      writes=[pk(*slot)] if dc == 7 else (), inc=(dc == 7))
                I("dve", lambda v, qc=qc, dest=dest, slot=slot: v.tensor_copy(
                    dest[:, qc * 512:(qc + 1) * 512], pap(*slot)),
                  reads=[pk(*slot)], writes=[dkey])
        for tt in range(16):
            slot = (1, "a") if tt % 2 == 0 else (1, "b")
            for dc in range(8):
                I("pe", lambda t, tt=tt, dc=dc, slot=slot: t.matmul(
                    pap(*slot, 128, 128), xn[:, dc * S + tt * 128: dc * S + tt * 128 + 128],
                    wb[2][:, dc * 128:(dc + 1) * 128],
                    start=(dc == 0), stop=(dc == 7)),
                  reads=["wb2"] + xnkeys, war=[pk(*slot)] if dc == 0 else (),
                  writes=[pk(*slot)] if dc == 7 else (), inc=(dc == 7))
            I("dve", lambda v, tt=tt, slot=slot: v.tensor_copy(
                v_sb[:, tt * 130:tt * 130 + 64], pap(*slot, 128, 128)[:, 0:64]),
              reads=[pk(*slot)], writes=["v"])
            I("dve", lambda v, tt=tt, slot=slot: v.tensor_copy(
                v_sb[:, tt * 130 + 65:tt * 130 + 129], pap(*slot, 128, 128)[:, 64:128]),
              reads=[pk(*slot)], writes=["v"])
        if debug and hp == 0:
            I("qact", lambda s: s.dma_start(out=dbg["q0"], in_=q_sb[:]), reads=["q"], writes=["dq", "dbgc"], dsem="dbgc")
            I("qact", lambda s: s.dma_start(out=dbg["k0"], in_=k_sb[:]), reads=["k"], writes=["dk", "dbgc"], dsem="dbgc")
            I("qact", lambda s: s.dma_start(out=dbg["v0"], in_=v_sb[:]),
              reads=["v", "vones", "vones2"], writes=["dv", "dbgc"], dsem="dbgc")
        for qc in range(4):
            zslot = [(0, "a"), (0, "b")] if qc % 2 == 0 else [(1, "a"), (1, "b")]
            def load_m(kt, qc):
                mpar = kt % 3
                I("qact", lambda s, kt=kt, qc=qc, mpar=mpar: s.dma_start(
                    out=m_sb[mpar][:],
                    in_=bass.AP(Md.tensor, kt * 262144 + qc * 512, [[2048, 128], [1, 512]])),
                  reads=["Md"], writes=[f"m{mpar}"])
            load_m(0, qc)
            load_m(1, qc)
            for kt in range(16):
                mpar = kt % 3
                sslot = [(2, "a"), (2, "b")] if kt % 2 == 0 else [(3, "a"), (3, "b")]
                if kt + 2 < 16:
                    load_m(kt + 2, qc)
                for hh in range(2):
                    ppar = 2 * hh + (kt % 2)
                    I("pe", lambda t, kt=kt, qc=qc, hh=hh, sslot=sslot: t.matmul(
                        pap(*sslot[hh]), k_sb[hh * 64:hh * 64 + 64, kt * 128:(kt + 1) * 128],
                        q_sb[hh * 64:hh * 64 + 64, qc * 512:(qc + 1) * 512],
                        start=True, stop=True, tile_position=(hh * 64, 0)),
                      reads=["k", "q"], writes=[pk(*sslot[hh])])
                    I("act", lambda a, hh=hh, ppar=ppar, sslot=sslot: a.activation(
                        pex[ppar][:], pap(*sslot[hh]), AF.Exp, scale=SCALE),
                      reads=[pk(*sslot[hh])], writes=[f"pex{ppar}"])
                    I("dve", lambda v, ppar=ppar, mpar=mpar: v.tensor_tensor(
                        pex[ppar][:], pex[ppar][:], m_sb[mpar][:], ALU.mult),
                      reads=[f"pex{ppar}", f"m{mpar}"], writes=[f"pex{ppar}"])
                    if debug and hp == 0 and qc == 0 and kt == 0 and hh == 0:
                        I("qact", lambda s: s.dma_start(out=dbg["p0"], in_=pex[0][:]),
                          reads=["pex0"], writes=["dp0", "dbgc"], dsem="dbgc")
                    I("pe", lambda t, kt=kt, hh=hh, ppar=ppar, zslot=zslot: t.matmul(
                        pap(*zslot[hh])[0:65, :], v_sb[:, kt * 130 + hh * 65: kt * 130 + hh * 65 + 65],
                        pex[ppar][:], start=(kt == 0), stop=(kt == 15)),
                      reads=["v", "vones", "vones2", f"pex{ppar}"],
                      war=[pk(*zslot[hh])] if kt == 0 else (),
                      writes=[pk(*zslot[hh])] if kt == 15 else (), inc=(kt == 15))
            for hh in range(2):
                I("dve", lambda v, qc=qc, hh=hh, zslot=zslot: v.tensor_copy(
                    z_sb[hh * 64:hh * 64 + 64, qc * 512:(qc + 1) * 512], pap(*zslot[hh])[0:64, :]),
                  reads=[pk(*zslot[hh])], writes=["z"])
                I("dve", lambda v, qc=qc, hh=hh, zslot=zslot: v.tensor_copy(
                    rowv[hh][:, qc * 512:(qc + 1) * 512], pap(*zslot[hh])[64:65, :]),
                  reads=[pk(*zslot[hh])], writes=[f"row{hh}"])
        I("dve", lambda v: v.reciprocal(rowv[0][:], rowv[0][:]),
          reads=["row0"], writes=["row0"])
        I("dve", lambda v: v.reciprocal(rowv[1][:], rowv[1][:]),
          reads=["row1"], writes=["row1"])
        I("qact", lambda s: s.dma_start(out=rowsd[2:3, :], in_=rowv[0][:]),
          reads=["row0"], writes=["rowsd2"], dsem="rowsd2")
        I("qact", lambda s: s.dma_start(out=rowsd[3:4, :], in_=rowv[1][:]),
          reads=["row1"], writes=["rowsd3"], dsem="rowsd3")
        I("qact", lambda s: s.dma_start(out=bc1[0:64, :], in_=rowsd[2:3, :].partition_broadcast(64)),
          reads=["rowsd2"], writes=["bc1"])
        I("qact", lambda s: s.dma_start(out=bc1[64:128, :], in_=rowsd[3:4, :].partition_broadcast(64)),
          reads=["rowsd3"], writes=["bc1b"])
        I("dve", lambda v: v.tensor_tensor(z_sb[:], z_sb[:], bc1[:], ALU.mult),
          reads=["z", "bc1", "bc1b"], writes=["z"])
        if debug and hp == 0:
            I("qact", lambda s: s.dma_start(out=dbg["z0"], in_=z_sb[:]), reads=["z"], writes=["dz", "dbgc"], dsem="dbgc")
        for oc in range(8):
            for tc in range(4):
                slot = (2, "a") if (oc * 4 + tc) % 2 == 0 else (2, "b")
                opar = (oc * 4 + tc) % 2
                I("pe", lambda t, oc=oc, tc=tc, slot=slot: t.matmul(
                    pap(*slot), wb[3][:, oc * 128:(oc + 1) * 128],
                    z_sb[:, tc * 512:(tc + 1) * 512], start=True, stop=True),
                  reads=["wo", "z"], writes=[pk(*slot)])
                I("dve", lambda v, slot=slot, opar=opar: v.tensor_copy(ost[opar][:], pap(*slot)),
                  reads=[pk(*slot)], writes=[f"ost{opar}"])
                I("qpool", lambda g, oc=oc, tc=tc, opar=opar: g.dma_start(
                    out=x2d[oc * 128:(oc + 1) * 128, tc * 512:(tc + 1) * 512],
                    in_=ost[opar][:], accum_op=ALU.add),
                  reads=[f"ost{opar}", f"x2:{oc}:{tc}"], writes=[f"x2:{oc}:{tc}"])

    b.phase(ATTNKEYS, LNKEYS)
    layer_norm(lambda dc: x2d[dc * 128:(dc + 1) * 128, :],
               lambda dc: [f"x2:{dc}:{tc}" for tc in range(4)], write_x2=False)
    if debug:
        for dc in range(8):
            I("qact", lambda s, dc=dc: s.dma_start(out=dbg["x2n"][dc * 128:(dc + 1) * 128, :],
                                                   in_=xn[:, dc * S:(dc + 1) * S]),
              reads=[f"xn{dc}"], writes=[f"dbgx2n{dc}", "dbgc"], dsem="dbgc")

    I("qsp", lambda s: s.dma_start(out=wb[0][:, 0:128].rearrange("p (c o) -> p c o", o=16),
                                   in_=bass.AP(gwT.tensor, 0, [[16, 128], [2048, 8], [1, 16]])),
      writes=["wb0"])
    for qc in range(4):
        slot = (3, "a") if qc % 2 == 0 else (3, "b")
        for dc in range(8):
            I("pe", lambda t, qc=qc, dc=dc, slot=slot: t.matmul(
                pap(*slot, 16, 512), wb[0][:, dc * 16:(dc + 1) * 16],
                xn[:, dc * S + qc * 512: dc * S + qc * 512 + 512],
                start=(dc == 0), stop=(dc == 7)),
              reads=["wb0"] + xnkeys, war=[pk(*slot)] if dc == 0 else (),
              writes=[pk(*slot)] if dc == 7 else (), inc=(dc == 7))
        I("act", lambda a, qc=qc, slot=slot: a.activation(
            sig[:, qc * 512:(qc + 1) * 512], pap(*slot, 16, 512), AF.Sigmoid),
          reads=[pk(*slot)], writes=["sig"])
    if debug:
        I("qact", lambda s: s.dma_start(out=dbg["sig"], in_=sig[:]), reads=["sig"], writes=["dsig", "dbgc"], dsem="dbgc")
    for tt in range(16):
        slot2 = (3, "a") if tt % 2 == 0 else (3, "b")
        I("pe", lambda t, tt=tt, slot2=slot2: t.transpose(
            pap(*slot2, 128, 16), sig[:, tt * 128:(tt + 1) * 128], idm[0:16, 0:16]),
          reads=["sig", "idm"], war=[pk(*slot2)], writes=[pk(*slot2)])
        I("dve", lambda v, tt=tt, slot2=slot2: v.tensor_copy(
            sigT[:, tt * 16:(tt + 1) * 16], pap(*slot2, 128, 16)),
          reads=[pk(*slot2)], writes=["sigT"])
    I("qact", lambda s: s.dma_start(out=st[3][:], in_=ebias.partition_broadcast(128)),
      writes=["st3"])
    for tt in range(16):
        I("dve", lambda v, tt=tt: v.tensor_tensor(
            selT[:, tt * 16:(tt + 1) * 16], sigT[:, tt * 16:(tt + 1) * 16], st[3][:], ALU.add),
          reads=["sigT", "st3"], writes=["selT"])
    I("dve", lambda v: v.memset(rank[:], 0.0), writes=["rank"])
    selT3 = selT[:].rearrange("p (t e) -> p t e", e=16)
    rank3 = rank[:].rearrange("p (t e) -> p t e", e=16)
    tmp33 = tmp3[:].rearrange("p (t e) -> p t e", e=16)
    for d in range(1, 16):
        n = 16 - d
        I("dve", lambda v, d=d, n=n: v.tensor_tensor(
            tmp33[:, :, 0:n], selT3[:, :, d:16], selT3[:, :, 0:n], ALU.is_gt),
          reads=["selT"], writes=["tmp3"])
        I("dve", lambda v, d=d, n=n: v.tensor_tensor(
            rank3[:, :, 0:n], rank3[:, :, 0:n], tmp33[:, :, 0:n], ALU.add),
          reads=["rank", "tmp3"], writes=["rank"])
        I("dve", lambda v, d=d, n=n: v.tensor_tensor(
            tmp33[:, :, n:16], selT3[:, :, 0:d], selT3[:, :, n:16], ALU.is_ge),
          reads=["selT"], writes=["tmp3"])
        I("dve", lambda v, d=d, n=n: v.tensor_tensor(
            rank3[:, :, n:16], rank3[:, :, n:16], tmp33[:, :, n:16], ALU.add),
          reads=["rank", "tmp3"], writes=["rank"])
    I("dve", lambda v: v.tensor_single_scalar(rank[:], rank[:], 8.0, ALU.is_lt),
      reads=["rank"], writes=["rank"])
    I("dve", lambda v: v.tensor_tensor(cmbT[:], sigT[:], rank[:], ALU.mult),
      reads=["sigT", "rank"], writes=["cmbT"])
    cmbT3 = cmbT[:].rearrange("p (t e) -> p t e", e=16)
    I("dve", lambda v: v.reduce_sum(st[2][:], cmbT3, axis=AX.X),
      reads=["cmbT"], writes=["st2"])
    I("dve", lambda v: v.reciprocal(st[2][:], st[2][:]), reads=["st2"], writes=["st2"])
    I("dve", lambda v: v.tensor_scalar(st[2][:], st[2][:], RSCALE, None, ALU.mult),
      reads=["st2"], writes=["st2"])
    for e in range(16):
        I("dve", lambda v, e=e: v.tensor_tensor(
            cmbT3[:, :, e], cmbT3[:, :, e], st[2][:], ALU.mult),
          reads=["cmbT", "st2"], writes=["cmbT"])
    for tt in range(16):
        slot = (2, "a") if tt % 2 == 0 else (2, "b")
        I("pe", lambda t, tt=tt, slot=slot: t.transpose(
            pap(*slot, 16, 128), cmbT[:, tt * 16:(tt + 1) * 16], idm[:]),
          reads=["cmbT", "idm"], war=[pk(*slot)], writes=[pk(*slot)])
        I("dve", lambda v, tt=tt, slot=slot: v.tensor_copy(cst[:], pap(*slot, 16, 128)),
          reads=[pk(*slot)], writes=["cst"])
        I("qact", lambda s, tt=tt: s.dma_start(
            out=cmbd[:, tt * 128:(tt + 1) * 128], in_=cst[:]),
          reads=["cst"], writes=[f"cmbd{tt}"], dsem="cmbd")

    b.phase(LNKEYS, FFNKEYS)
    hkeys = [f"h{ic}" for ic in range(8)]
    for tc in range(4):
        I("qsp", lambda s, tc=tc: s.dma_start(
            out=acc.rearrange("p (c f) -> p c f", f=512),
            in_=bass.AP(x2d.tensor, tc * 512, [[2048, 128], [262144, 8], [1, 512]])),
          reads=[f"x2:{dc}:{tc}" for dc in range(8)], writes=["acc"])
        for e in range(17):
            shared = (e == 16)
            wpar = e % 2
            wgk, wuk = f"wb{wpar}", f"wb{2 + wpar}"
            if not shared:
                I("qact", lambda s, e=e, tc=tc: s.dma_start(
                    out=cb[e % 2][:], in_=cmbd[e:e + 1, tc * 512:(tc + 1) * 512].partition_broadcast(128)),
                  reads=[f"cmbd{t2}" for t2 in range(16)], writes=[f"cb{e % 2}"])
            for ic in range(8):
                if shared:
                    I("qsp", lambda s, ic=ic: s.dma_start(
                        out=wb[0][:].rearrange("p (c o) -> p c o", o=128),
                        in_=bass.AP(sgT.tensor, ic * 128, [[1024, 128], [131072, 8], [1, 128]])),
                      writes=["wb0"])
                    I("qsp", lambda s, ic=ic: s.dma_start(
                        out=wb[2][:].rearrange("p (c o) -> p c o", o=128),
                        in_=bass.AP(suT.tensor, ic * 128, [[1024, 128], [131072, 8], [1, 128]])),
                      writes=["wb2"])
                    wgk2, wuk2 = "wb0", "wb2"
                else:
                    I("qsp", lambda s, e=e, ic=ic, wpar=wpar: s.dma_start(
                        out=wb[wpar][:].rearrange("p (c o) -> p c o", o=128),
                        in_=bass.AP(wgT.tensor, e * D * II + ic * 128, [[1024, 128], [131072, 8], [1, 128]])),
                      writes=[wgk])
                    I("qsp", lambda s, e=e, ic=ic, wpar=wpar: s.dma_start(
                        out=wb[2 + wpar][:].rearrange("p (c o) -> p c o", o=128),
                        in_=bass.AP(wuT.tensor, e * D * II + ic * 128, [[1024, 128], [131072, 8], [1, 128]])),
                      writes=[wuk])
                    wgk2, wuk2 = wgk, wuk
                gslot = (0, "a") if ic % 2 == 0 else (1, "a")
                uslot = (0, "b") if ic % 2 == 0 else (1, "b")
                for dc in range(8):
                    I("pe", lambda t, tc=tc, ic=ic, dc=dc, gslot=gslot, wgk2=wgk2: t.matmul(
                        pap(*gslot), wb[{"wb0": 0, "wb1": 1, "wb2": 2, "wb3": 3}[wgk2]][:, dc * 128:(dc + 1) * 128],
                        xn[:, dc * S + tc * 512: dc * S + tc * 512 + 512],
                        start=(dc == 0), stop=(dc == 7)),
                      reads=[wgk2] + xnkeys, war=[pk(*gslot)] if dc == 0 else (),
                      writes=[pk(*gslot)] if dc == 7 else (), inc=(dc == 7))
                for dc in range(8):
                    I("pe", lambda t, tc=tc, ic=ic, dc=dc, uslot=uslot, wuk2=wuk2: t.matmul(
                        pap(*uslot), wb[{"wb0": 0, "wb1": 1, "wb2": 2, "wb3": 3}[wuk2]][:, dc * 128:(dc + 1) * 128],
                        xn[:, dc * S + tc * 512: dc * S + tc * 512 + 512],
                        start=(dc == 0), stop=(dc == 7)),
                      reads=[wuk2] + xnkeys, war=[pk(*uslot)] if dc == 0 else (),
                      writes=[pk(*uslot)] if dc == 7 else (), inc=(dc == 7))
                hk = hkeys[ic]
                I("act", lambda a, ic=ic, gslot=gslot: a.activation(
                    h_sb[:, ic * 512:(ic + 1) * 512], pap(*gslot), AF.Silu),
                  reads=[pk(*gslot)], writes=[hk])
                I("dve", lambda v, ic=ic, uslot=uslot: v.tensor_tensor(
                    h_sb[:, ic * 512:(ic + 1) * 512], h_sb[:, ic * 512:(ic + 1) * 512],
                    pap(*uslot), ALU.mult),
                  reads=[hk, pk(*uslot)], writes=[hk])
                if not shared:
                    I("dve", lambda v, ic=ic, e=e: v.tensor_tensor(
                        h_sb[:, ic * 512:(ic + 1) * 512], h_sb[:, ic * 512:(ic + 1) * 512],
                        cb[e % 2][:], ALU.mult),
                      reads=[hk, f"cb{e % 2}"], writes=[hk])
            for dc in range(8):
                wdk = f"wb{4 + dc % 2}"
                if shared:
                    I("qsp", lambda s, dc=dc, wdk=wdk: s.dma_start(
                        out=wb[4 + dc % 2][:].rearrange("p (c o) -> p c o", o=128),
                        in_=bass.AP(sdT.tensor, dc * 128, [[1024, 128], [131072, 8], [1, 128]])),
                      writes=[wdk])
                else:
                    I("qsp", lambda s, e=e, dc=dc, wdk=wdk: s.dma_start(
                        out=wb[4 + dc % 2][:].rearrange("p (c o) -> p c o", o=128),
                        in_=bass.AP(wdT.tensor, e * II * D + dc * 128, [[1024, 128], [131072, 8], [1, 128]])),
                      writes=[wdk])
                yslot = (2, "a") if dc % 2 == 0 else (2, "b")
                for ic in range(8):
                    I("pe", lambda t, dc=dc, ic=ic, yslot=yslot, wdk=wdk: t.matmul(
                        pap(*yslot), wb[4 + dc % 2][:, ic * 128:(ic + 1) * 128],
                        h_sb[:, ic * 512:(ic + 1) * 512],
                        start=(ic == 0), stop=(ic == 7)),
                      reads=[wdk] + hkeys, war=[pk(*yslot)] if ic == 0 else (),
                      writes=[pk(*yslot)] if ic == 7 else (), inc=(ic == 7))
                I("dve", lambda v, dc=dc, yslot=yslot: v.tensor_tensor(
                    acc[:, dc * 512:(dc + 1) * 512], acc[:, dc * 512:(dc + 1) * 512],
                    pap(*yslot), ALU.add),
                  reads=["acc", pk(*yslot)], writes=["acc"])
        I("qact", lambda s, tc=tc: s.dma_start(
            out=bass.AP(y_fm.tensor, tc * 512, [[2048, 128], [262144, 8], [1, 512]]),
            in_=acc.rearrange("p (c f) -> p c f", f=512)),
          reads=["acc"], writes=[f"y{tc}"])

    b.barrier_all_dma()

    with b.nc.Block() as block:
        @block.sync
        def _(eng):
            for op in b.plan["sp"]:
                op(eng)

        @block.tensor
        def _(eng):
            for op in b.plan["pe"]:
                op(eng)

        @block.vector
        def _(eng):
            for op in b.plan["dve"]:
                op(eng)

        @block.scalar
        def _(eng):
            for op in b.plan["act"]:
                op(eng)

        @block.gpsimd
        def _(eng):
            for op in b.plan["pool"]:
                op(eng)

    b.es.close()
    return b.nc


def _host_inputs(inputs):
    x = np.asarray(inputs["L0_attn_x"], dtype=np.float32)
    minds = np.ascontiguousarray(np.asarray(inputs["L0_attn_mask_inds"], dtype=np.int32))
    c = np.ascontiguousarray
    common = {
        "minds": minds,
        "wqT": c(np.asarray(inputs["L0_attn_Wq"], dtype=np.float32).T),
        "wkT": c(np.asarray(inputs["L0_attn_Wk"], dtype=np.float32).T),
        "wvT": c(np.asarray(inputs["L0_attn_Wv"], dtype=np.float32).T),
        "woT": c(np.asarray(inputs["L0_attn_Wo"], dtype=np.float32).T),
        "gwT": c(np.asarray(inputs["L0_ffn_gate_w"], dtype=np.float32).T),
        "ebias": c(np.asarray(inputs["L0_ffn_expert_bias"], dtype=np.float32).reshape(1, E)),
        "wgT": c(np.asarray(inputs["L0_ffn_w_gate"], dtype=np.float32).transpose(0, 2, 1)),
        "wuT": c(np.asarray(inputs["L0_ffn_w_up"], dtype=np.float32).transpose(0, 2, 1)),
        "wdT": c(np.asarray(inputs["L0_ffn_w_down"], dtype=np.float32).transpose(0, 2, 1)),
        "sgT": c(np.asarray(inputs["L0_ffn_shared_gate_w"], dtype=np.float32).T),
        "suT": c(np.asarray(inputs["L0_ffn_shared_up_w"], dtype=np.float32).T),
        "sdT": c(np.asarray(inputs["L0_ffn_shared_down_w"], dtype=np.float32).T),
    }
    in_maps = []
    for core in range(8):
        m = dict(common)
        m["xin"] = c(x[core].T)
        in_maps.append(m)
    return in_maps


def kernel(**inputs):
    in_maps = _host_inputs(inputs)
    nc = build(debug=False)
    res = run_bass_kernel_spmd(nc, in_maps, core_ids=list(range(8)))
    out = np.stack([np.asarray(r["y_fm"]).T for r in res.results], axis=0)
    return np.ascontiguousarray(out.astype(np.float32))
